# version 07
# baseline (speedup 1.0000x reference)
import math
import sys
import os
SPLITQ = 0
import numpy as np
import ml_dtypes
import concourse.bass as bass
import concourse.mybir as mybir
from concourse.bass_utils import run_bass_kernel_spmd

F32 = mybir.dt.float32
BF16 = mybir.dt.bfloat16
AF = mybir.ActivationFunctionType
ALU = mybir.AluOpType
AX = mybir.AxisListType

D = 1024
HID = 4096
NB = 8192
NO = 4096
ICH = D // 128
KCH = NB // 128
QCH = NO // 128
EPS = 1e-5
LN09 = math.log(0.9)
LN4 = math.log(4.0)

ENG = ("sync", "gpsimd", "vector", "scalar", "tensor")


class Sem:
    def __init__(self, st, nc, name):
        self.h = st.enter_context(nc.semaphore(name))
        self.n = 0
        self.name = name


class Prog:

    def __init__(self):
        self.q = {e: [] for e in ENG}
        self.ninst = 0

    def do(self, e, f, waits=(), inc=None, inck=1):
        waits = tuple((s, v) for (s, v) in waits)

        def g(eng, f=f, waits=waits, inc=inc, inck=inck):
            for (s, v) in waits:
                eng.wait_ge(s.h, v)
            ins = f(eng)
            if inc is not None:
                assert ins is not None
                ins.then_inc(inc.h, inck)

        self.q[e].append(g)
        self.ninst += 1 + len(waits)
        if inc is not None:
            inc.n += inck
            return inc.n
        return None

    def wait(self, e, *pairs):
        self.do(e, lambda eng: None, waits=pairs)


class Arena:

    def __init__(self, ap, total):
        self.ap = ap
        self.total = total
        self.top = 0
        self.hwm = 0

    def _take(self, n):
        off = self.top
        n = (n + 63) & ~63
        self.top += n
        self.hwm = max(self.hwm, self.top)
        assert self.top <= self.total, f"arena overflow {self.top} > {self.total}"
        return off

    def bf(self, n):
        off = self._take(n)
        return self.ap[:, off:off + n]

    def f32(self, n):
        off = self._take(2 * n)
        return self.ap[:, off:off + 2 * n].bitcast(F32)

    def mark(self):
        return self.top

    def reset(self, mark):
        self.top = mark


def ceil16(v):
    return ((v + 15) // 16) * 16


def build(stage):
    nc = bass.Bass("TRN2", target_bir_lowering=False)
    from contextlib import ExitStack
    st = ExitStack()

    def ein(name, shape, dt=F32):
        return nc.dram_tensor(name, shape, dt, kind="ExternalInput").ap()

    def eout(name, shape, dt=F32):
        return nc.dram_tensor(name, shape, dt, kind="ExternalOutput").ap()

    def int_d(name, shape, dt):
        return nc.dram_tensor(name, shape, dt, kind="Internal").ap()

    x_own = ein("x_own", [NO, D])
    x_oth = ein("x_oth", [NO, D])
    wq_i = ein("wq", [D, D]); wk_i = ein("wk", [D, D])
    wv_i = ein("wv", [D, D]); wo_i = ein("wo", [D, D])
    rows_i = {}
    for j in ("q", "k", "v"):
        for p in ("g", "b", "m", "v"):
            rows_i[p + j] = ein(p + j, [1, D])
    bo_i = ein("bo", [1, D])
    w1_i = ein("w1", [HID, D]); w2_i = ein("w2", [HID, D])
    h1_i = ein("h1", [HID, HID]); h2_i = ein("h2", [HID, HID])
    w3_i = ein("w3", [D, HID])
    s1_i = ein("s1", [32, 128]); s2_i = ein("s2", [32, 128])
    sh1_i = ein("sh1", [32, 128]); sh2_i = ein("sh2", [32, 128])
    s3_i = ein("s3", [8, 128])

    xn1_d = int_d("xn1_d", [NB, D], BF16)
    sq_d = int_d("sq_d", [1, NO], F32)
    prow_d = int_d("prow_d", [8, D], F32)
    out1_d = int_d("out1_d", [NO, D], F32)
    xn3_d = int_d("xn3_d", [NO, D], BF16)
    w1c = int_d("w1c", [HID, D], BF16); w2c = int_d("w2c", [HID, D], BF16)
    h1c = int_d("h1c", [HID, HID], BF16); h2c = int_d("h2c", [HID, HID], BF16)
    w3c = int_d("w3c", [D, HID], BF16)
    h_d = int_d("h_d", [HID, NO], BF16)
    g_d = int_d("g_d", [HID, NO], BF16)
    k_d = int_d("k_d", [HID, NO], BF16)

    out_o = eout("out", [NO, D])

    dbg = {}
    if stage == "a1a":
        dbg["xn1"] = eout("dbg_xn1", [NB, D], BF16)
    if stage == "a1b":
        dbg["skcol"] = eout("dbg_skcol", [128, KCH])
        dbg["sqcol"] = eout("dbg_sqcol", [128, QCH])
        dbg["vsb"] = eout("dbg_vsb", [128, 2 * D], BF16)
        dbg["vev"] = eout("dbg_vev", [128, D])
        dbg["pbv"] = eout("dbg_pbv", [128, D])
    if stage == "a2":
        dbg["out1"] = eout("dbg_out1", [NO, D])
    if stage == "dbgv":
        dbg["v4"] = eout("dbg_v4", [128, 64 * 128], BF16)
        dbg["vsb0"] = eout("dbg_vsb0", [128, D], BF16)
        dbg["pbv"] = eout("dbg_pbv", [128, D])
        dbg["bo"] = eout("dbg_bo", [128, D])
        dbg["csh"] = eout("dbg_csh", [128, 16], BF16)
        dbg["wlo7"] = eout("dbg_wlo7", [128, D], BF16)
        dbg["brow"] = eout("dbg_brow", [1, D], BF16)
        dbg["wsb7"] = eout("dbg_wsb7", [128, D], BF16)
        dbg["cs"] = eout("dbg_cs", [128, 8])
        dbg["sk"] = eout("dbg_sk", [128, KCH])
        dbg["sq"] = eout("dbg_sq", [128, QCH])
    if stage == "bdbg":
        dbg["h"] = eout("dbg_h", [256, NO], BF16)
        dbg["k"] = eout("dbg_k", [256, NO], BF16)
        dbg["g"] = eout("dbg_g", [256, NO], BF16)
        dbg["xn3"] = eout("dbg_xn3", [128, D], BF16)
        dbg["out1"] = eout("dbg_out1", [256, D])
    if stage == "a2d":
        dbg["avsb"] = eout("dbg_avsb", [128, 4096], BF16)
        dbg["sqrow"] = eout("dbg_sqrow", [1, NO])
        dbg["out1"] = eout("dbg_out1", [512, D])
        dbg["woffbo"] = eout("dbg_woffbo", [128, D])
        dbg["sct63"] = eout("dbg_sct63", [128, 512], BF16)
        dbg["absd63"] = eout("dbg_absd63", [128, 512], BF16)
        dbg["p63"] = eout("dbg_p63", [128, 512], BF16)
        dbg["sqb0"] = eout("dbg_sqb0", [128, 512])

    ATOT = 106368
    abuf = st.enter_context(nc.sbuf_tensor("arena", [128, ATOT], BF16))
    A = Arena(abuf[:], ATOT)

    bigbuf = A.bf(65536)
    wtg = [A.bf(8192), A.bf(8192)]
    scorep = A.bf(3072)
    scr1k = [A.f32(1024), A.f32(1024)]
    sqscr = A.f32(1024)
    eyef = A.f32(128)
    smalls = A.f32(512)
    onescol_bf = A.bf(64)

    statc = [smalls[:, 0:16], smalls[:, 16:32]]
    epscol = smalls[:, 32:33]
    ln09col = smalls[:, 33:34]
    s1col = smalls[:, 64:96]
    s2col = smalls[:, 96:128]
    sh1col = smalls[:, 128:160]
    sh2col = smalls[:, 160:192]
    s3col = smalls[:, 192:200]
    cs09col = smalls[:, 200:208]
    skcol = smalls[:, 224:288]
    sqcol = smalls[:, 288:320]
    lncol = [smalls[:, 320:336], smalls[:, 336:352]]
    icol_f = smalls[:, 352:353]

    pool_mark = A.mark()

    pb = [st.enter_context(nc.psum_tensor(f"pb{i}", [128, 512], F32)) for i in range(8)]
    pba = [p[:] for p in pb]

    ds1 = Sem(st, nc, "ds1")
    ds2 = Sem(st, nc, "ds2")
    gs1 = Sem(st, nc, "gs1")
    gs2 = Sem(st, nc, "gs2")
    vs = Sem(st, nc, "vs")
    ss = Sem(st, nc, "ss")
    ts = Sem(st, nc, "ts")
    osem = Sem(st, nc, "osem")
    sds = Sem(st, nc, "sds")

    P = Prog()


    def tmpf32(n, off=[pool_mark]):
        o = off[0]
        off[0] += 2 * ((n + 31) & ~31)
        assert off[0] < ATOT
        return A.ap[:, o:o + 2 * n].bitcast(F32)

    W0 = {}
    W0["irow"] = tmpf32(128)
    W0["icol"] = tmpf32(32)
    W0["rowa"] = tmpf32(1024)
    W0["rowb"] = tmpf32(1024)
    W0["rowc"] = tmpf32(1024)
    W0["s32"] = tmpf32(128)

    P.do("gpsimd", lambda e: e.iota(W0["irow"], [[1, 128]], base=0, channel_multiplier=0,
                                    allow_small_or_imprecise_dtypes=True), inc=gs1, inck=1)
    P.do("gpsimd", lambda e: e.iota(W0["icol"][:, 0:1], [[0, 1]], base=0, channel_multiplier=1,
                                    allow_small_or_imprecise_dtypes=True), inc=gs1, inck=1)
    gs1_iota = gs1.n
    P.do("vector", lambda e: e.memset(epscol, EPS))
    P.do("vector", lambda e: e.memset(ln09col, LN09))
    P.do("vector", lambda e: e.memset(onescol_bf[:, 0:1], 1.0))
    v_eye = P.do("vector", lambda e: e.tensor_single_scalar(eyef, W0["irow"], W0["icol"][:, 0:1], ALU.is_equal),
                 waits=[(gs1, gs1_iota)], inc=vs)

    for ji, j in enumerate(("q", "k", "v")):
        ra, rb, rc = W0["rowa"][0:1, :], W0["rowb"][0:1, :], W0["rowc"][0:1, :]
        d1 = P.do("gpsimd", lambda e, j=j: e.dma_start(out=W0["rowa"][0:1, :], in_=rows_i["v" + j]), inc=gs1, inck=16)
        c1 = P.do("scalar", lambda e: e.activation(W0["rowa"][0:1, :], W0["rowa"][0:1, :], AF.Ln, bias=epscol[0:1, 0:1]),
                  waits=[(gs1, d1)], inc=ss)
        ln4c = epscol
        P.do("vector", lambda e: e.memset(lncol[0][:, 0:1], LN4))
        c2 = P.do("scalar", lambda e: e.activation(W0["rowa"][0:1, :], W0["rowa"][0:1, :], AF.Exp,
                                                   scale=-0.5, bias=lncol[0][0:1, 0:1]),
                  waits=[(ss, c1)], inc=ss)
        d2 = P.do("gpsimd", lambda e, j=j: e.dma_start(out=W0["rowb"][0:1, :], in_=rows_i["g" + j]), inc=gs1, inck=16)
        v1 = P.do("vector", lambda e: e.tensor_tensor(W0["rowa"][0:1, :], W0["rowa"][0:1, :], W0["rowb"][0:1, :], ALU.mult),
                  waits=[(ss, c2), (gs1, d2)], inc=vs)
        d3 = P.do("gpsimd", lambda e, ji=ji: e.dma_start(out=prow_d[ji:ji + 1, :], in_=W0["rowa"][0:1, :]),
                  waits=[(vs, v1)], inc=gs1, inck=16)
        d4 = P.do("gpsimd", lambda e, j=j: e.dma_start(out=W0["rowb"][0:1, :], in_=rows_i["m" + j]), inc=gs1, inck=16)
        v2 = P.do("vector", lambda e: e.tensor_tensor(W0["rowb"][0:1, :], W0["rowb"][0:1, :], W0["rowa"][0:1, :], ALU.mult),
                  waits=[(vs, v1), (gs1, d4)], inc=vs)
        d5 = P.do("gpsimd", lambda e, j=j: e.dma_start(out=W0["rowc"][0:1, :], in_=rows_i["b" + j]), inc=gs1, inck=16)
        v3 = P.do("vector", lambda e: e.scalar_tensor_tensor(W0["rowc"][0:1, :], W0["rowc"][0:1, :], 4.0,
                                                             W0["rowb"][0:1, :], ALU.mult, ALU.subtract),
                  waits=[(vs, v2), (gs1, d5)], inc=vs)
        v4 = P.do("vector", lambda e: e.tensor_scalar_add(W0["rowc"][0:1, :], W0["rowc"][0:1, :], -4.0),
                  waits=[(vs, v3)], inc=vs)
        P.do("gpsimd", lambda e, ji=ji: e.dma_start(out=prow_d[3 + ji:4 + ji, :], in_=W0["rowc"][0:1, :]),
             waits=[(vs, v4)], inc=gs1, inck=16)
    gs_prow = gs1.n

    prev_cp = 0
    prev_tp = 0
    for (src, dst, npart) in ((s1_i, s1col, 32), (s2_i, s2col, 32),
                              (sh1_i, sh1col, 32), (sh2_i, sh2col, 32), (s3_i, s3col, 8)):
        d1 = P.do("gpsimd", lambda e, src=src, npart=npart: e.dma_start(out=W0["s32"][0:npart, :], in_=src),
                  waits=[(ts, prev_tp)], inc=gs1, inck=16)
        t1 = P.do("tensor", lambda e, npart=npart: e.transpose(pba[7][:, 0:npart], W0["s32"][0:npart, :],
                                                               eyef[0:npart, 0:npart]),
                  waits=[(gs1, d1), (vs, v_eye), (vs, prev_cp)], inc=ts)
        prev_tp = t1
        prev_cp = P.do("vector", lambda e, dst=dst, npart=npart: e.tensor_copy(dst[:, 0:npart], pba[7][:, 0:npart]),
             waits=[(ts, t1)], inc=vs)
    vs_scols = vs.n
    ts_scols = ts.n


    A.reset(pool_mark)
    _xbt = bigbuf[:, 57344:65536].bitcast(F32)
    xinbf = [_xbt[:, i * 1024:(i + 1) * 1024] for i in range(4)]
    xn1t = [scorep[:, 0:1024], scorep[:, 1024:2048]]
    pool_a1a = A.mark()

    xn1_wr = [0] * KCH
    ld_ms = [0] * KCH
    norm_ms = [0] * KCH
    sq_ms = [0] * KCH

    for i in range(KCH):
        src = x_own if i < QCH else x_oth
        r0 = (i % QCH) * 128
        slot = i % 4
        w = []
        if i >= 4:
            w = [(ss, sq_ms[i - 4]), (vs, norm_ms[i - 4])]
        ld_ms[i] = P.do("sync", lambda e, src=src, r0=r0, slot=slot: e.dma_start(
            out=xinbf[slot], in_=src[r0:r0 + 128, :]), waits=w, inc=ds1, inck=16)
        sc = statc[i % 2]
        w = [(ds1, ld_ms[i])]
        if i >= 2:
            w.append((vs, norm_ms[i - 2]))
            w.append((gs1, xn1_wr[i - 2]))
        sq_ms[i] = P.do("scalar", lambda e, slot=slot, i=i, sc=sc: e.activation(
            xn1t[i % 2], xinbf[slot], AF.Square, accum_out=sc[:, 1:2]), waits=w, inc=ss)
        v1 = P.do("vector", lambda e, slot=slot, sc=sc: e.reduce_sum(sc[:, 0:1], xinbf[slot], AX.X),
                  waits=[(ds1, ld_ms[i])] + ([(ss, sq_ms[i - 2])] if i >= 2 else []), inc=vs)
        v2 = P.do("vector", lambda e, sc=sc: e.tensor_single_scalar(sc[:, 2:3], sc[:, 0:1], 1.0 / D, ALU.mult),
                  waits=[(vs, v1)], inc=vs)
        v3 = P.do("vector", lambda e, sc=sc: e.tensor_tensor(sc[:, 3:4], sc[:, 2:3], sc[:, 2:3], ALU.mult),
                  waits=[(vs, v2)], inc=vs)
        v4 = P.do("vector", lambda e, sc=sc: e.scalar_tensor_tensor(sc[:, 4:5], sc[:, 1:2], 1.0 / D,
                                                                    sc[:, 3:4], ALU.mult, ALU.subtract),
                  waits=[(vs, v3), (ss, sq_ms[i])], inc=vs)
        c1 = P.do("scalar", lambda e, sc=sc: e.activation(sc[:, 5:6], sc[:, 4:5], AF.Ln, bias=epscol[:, 0:1]),
                  waits=[(vs, v4)], inc=ss)
        c2 = P.do("scalar", lambda e, sc=sc: e.activation(sc[:, 6:7], sc[:, 5:6], AF.Exp, scale=-0.5),
                  waits=[(ss, c1)], inc=ss)
        norm_ms[i] = P.do("vector", lambda e, slot=slot, i=i, sc=sc: e.tensor_scalar(
            xn1t[i % 2], xinbf[slot], sc[:, 2:3], sc[:, 6:7], ALU.subtract, ALU.mult),
            waits=[(ss, c2)], inc=vs)
        xn1_wr[i] = P.do("gpsimd", lambda e, i=i: e.dma_start(out=xn1_d[i * 128:(i + 1) * 128, :], in_=xn1t[i % 2]),
                         waits=[(vs, norm_ms[i])], inc=gs1, inck=16)


    if stage == "a1a":
        for i in range(KCH):
            d1 = P.do("sync", lambda e, i=i: e.dma_start(out=xn1t[i % 2], in_=xn1_d[i * 128:(i + 1) * 128, :]),
                      waits=[(gs1, xn1_wr[KCH - 1])] + ([(osem, 16 * (i - 1))] if i >= 2 else []), inc=ds1, inck=16)
            P.do("sync", lambda e, i=i: e.dma_start(out=dbg["xn1"][i * 128:(i + 1) * 128, :], in_=xn1t[i % 2]),
                 waits=[(ds1, d1)], inc=osem, inck=16)
        P.wait("sync", (osem, 16 * KCH))
        finish(nc, st, P)
        return nc



    cast_ms = {}
    ffn_cast_specs = {"w1": (w1_i, w1c, HID, D), "w2": (w2_i, w2c, HID, D),
                      "h1": (h1_i, h1c, HID, HID), "h2": (h2_i, h2c, HID, HID),
                      "w3": (w3_i, w3c, D, HID)}

    def emit_casts(names):
        for name in names:
            srcd, dstd, rows, cols = ffn_cast_specs[name]
            for ci in range(rows // 512):
                P.do("gpsimd", lambda e, srcd=srcd, dstd=dstd, ci=ci: e.dma_start(
                    out=dstd[ci * 512:(ci + 1) * 512, :], in_=srcd[ci * 512:(ci + 1) * 512, :]),
                    inc=gs1, inck=16)
            cast_ms[name] = gs1.n

    A.reset(pool_mark)
    xnfm = [A.bf(2048), A.bf(2048)]
    wpre_b = A.f32(1024)
    pbias_b = A.f32(1024)
    vevac = [A.f32(1024), A.f32(1024)]
    sigdump = A.bf(1024)
    pool_a1b = A.mark()
    sigscr = [sigdump, sigdump]

    wsb = wtg[0]
    wsb2 = wtg[1]

    def fill_bcast(dst_f32, src_row_ap, wait_sems):
        return P.do("sync", lambda e: e.dma_start(out=dst_f32, in_=src_row_ap.broadcast_to([128, D])),
                    waits=wait_sems, inc=ds1, inck=16)

    def load_w(wt_dst, w_src, wait_extra=()):
        ms = 0
        for c in range(ICH):
            ms = P.do("gpsimd", lambda e, c=c, wt_dst=wt_dst, w_src=w_src: e.dma_start(
                out=wt_dst[:, c * 1024:(c + 1) * 1024], in_=w_src[c * 128:(c + 1) * 128, :]),
                waits=wait_extra if c == 0 else (), inc=gs1, inck=16)
        return ms

    def prescale(wt, pre_b, dma_ms, pre_ms):
        ms = 0
        for c in range(ICH):
            ms = P.do("vector", lambda e, c=c, wt=wt, pre_b=pre_b: e.tensor_tensor(
                wt[:, c * 1024:(c + 1) * 1024], wt[:, c * 1024:(c + 1) * 1024], pre_b, ALU.mult),
                waits=[(gs1, dma_ms), (ds1, pre_ms)], inc=vs)
        return ms

    dq = load_w(wsb, wq_i)
    emit_casts(["w1", "w2"])
    pq = fill_bcast(wpre_b, prow_d[0:1, :], [(gs1, gs_prow)])
    vq_ps = prescale(wsb, wpre_b, dq, pq)
    pbq = fill_bcast(pbias_b, prow_d[3:4, :], [(vs, vq_ps), (ts, ts_scols)])

    tdma_ms = {}
    evac_ms = {}
    sig_ms = {}
    tdma_consumed_q = {}
    NSC_Q = NO // 256

    for sc in range(NSC_Q):
        w = [(gs1, xn1_wr[(sc + 1) * 2 - 1])]
        if sc >= 2:
            w.append((ts, tdma_consumed_q[sc - 2]))
        for ic in range(ICH):
            tdma_ms[sc] = P.do("sync", lambda e, sc=sc, ic=ic: e.dma_start(
                out=xnfm[sc % 2][:, ic * 256:(ic + 1) * 256],
                in_=xn1_d[sc * 256:(sc + 1) * 256, ic * 128:(ic + 1) * 128], transpose=True),
                waits=w if ic == 0 else (), inc=ds1, inck=16)
        for tc in range(2):
            qc = sc * 2 + tc
            bk = (qc % 2) * 2
            wmm = [(ds1, tdma_ms[sc]), (vs, vq_ps)]
            if qc >= 2:
                wmm.append((vs, evac_ms[qc - 2]))
            gq_ms = [0, 0]
            for half in range(2):
                for ic in range(ICH):
                    gq_ms[half] = P.do("tensor", lambda e, sc=sc, tc=tc, ic=ic, half=half, bk=bk: e.matmul(
                        pba[bk + half], xnfm[sc % 2][:, ic * 256 + tc * 128: ic * 256 + (tc + 1) * 128],
                        wsb[:, ic * 1024 + half * 512: ic * 1024 + (half + 1) * 512],
                        start=(ic == 0), stop=(ic == ICH - 1)),
                        waits=wmm if (half == 0 and ic == 0) else (), inc=(ts if ic == ICH - 1 else None))
            wev = [(ts, gq_ms[0]), (ds1, pbq)]
            if qc >= 2:
                wev.append((ss, sig_ms[qc - 2]))
            e0 = P.do("vector", lambda e, qc=qc, bk=bk: e.tensor_tensor(
                scr1k[qc % 2][:, 0:512], pba[bk], pbias_b[:, 0:512], ALU.add), waits=wev, inc=vs)
            evac_ms[qc] = P.do("vector", lambda e, qc=qc, bk=bk: e.tensor_tensor(
                scr1k[qc % 2][:, 512:1024], pba[bk + 1], pbias_b[:, 512:1024], ALU.add),
                waits=[(ts, gq_ms[1])], inc=vs)
            sig_ms[qc] = P.do("scalar", lambda e, qc=qc: e.activation(
                sigscr[qc % 2], scr1k[qc % 2], AF.Sigmoid, accum_out=sqcol[:, qc:qc + 1]),
                waits=[(vs, evac_ms[qc])], inc=ss)
        tdma_consumed_q[sc] = gq_ms[1]
    sig_q_done = ss.n

    dk = load_w(wsb, wk_i, wait_extra=[(ts, gq_ms[1])])
    pk = fill_bcast(wpre_b, prow_d[1:2, :], [(vs, evac_ms[NO // 128 - 1])])
    vk_ps = prescale(wsb, wpre_b, dk, pk)
    dv = load_w(wsb2, wv_i)
    emit_casts(["h1"])
    pv = fill_bcast(wpre_b, prow_d[2:3, :], [(vs, vk_ps)])
    vv_ps = prescale(wsb2, wpre_b, dv, pv)
    pbk = fill_bcast(pbias_b, prow_d[4:5, :], [(ss, sig_q_done), (ts, ts_scols)])
    pbv = fill_bcast(wpre_b, prow_d[5:6, :], [(vs, vv_ps)])

    NSC_K = NB // 256
    evk_ms = {}
    evv_ms = {}
    sigk_ms = {}
    sigv_ms = {}
    kv_last_mm = {}
    for sc in range(NSC_K):
        w = [(gs1, xn1_wr[min(KCH - 1, (sc + 1) * 2 - 1)])]
        if sc >= 2:
            w.append((ts, kv_last_mm[sc - 2]))
        for ic in range(ICH):
            tdma_ms[sc] = P.do("sync", lambda e, sc=sc, ic=ic: e.dma_start(
                out=xnfm[sc % 2][:, ic * 256:(ic + 1) * 256],
                in_=xn1_d[sc * 256:(sc + 1) * 256, ic * 128:(ic + 1) * 128], transpose=True),
                waits=w if ic == 0 else (), inc=ds1, inck=16)
        for tc in range(2):
            kc = sc * 2 + tc
            bk = (kc % 2) * 2
            bv = 4 + (kc % 2) * 2
            wmm = [(ds1, tdma_ms[sc]), (vs, vk_ps)]
            if kc == 0:
                wmm.append((vs, evac_ms[NO // 128 - 1]))
            if kc >= 2:
                wmm.append((vs, evk_ms[kc - 2]))
            gk_ms = [0, 0]
            for half in range(2):
                for ic in range(ICH):
                    gk_ms[half] = P.do("tensor", lambda e, sc=sc, tc=tc, ic=ic, half=half, bk=bk: e.matmul(
                        pba[bk + half], xnfm[sc % 2][:, ic * 256 + tc * 128: ic * 256 + (tc + 1) * 128],
                        wsb[:, ic * 1024 + half * 512: ic * 1024 + (half + 1) * 512],
                        start=(ic == 0), stop=(ic == ICH - 1)),
                        waits=wmm if (half == 0 and ic == 0) else (), inc=(ts if ic == ICH - 1 else None))
            wmmv = [(vs, vv_ps)]
            if kc == 0:
                wmmv.append((vs, vs_scols))
            if kc >= 2:
                wmmv.append((vs, evv_ms[kc - 2]))
            gv_ms = [0, 0]
            for half in range(2):
                for ic in range(ICH):
                    gv_ms[half] = P.do("tensor", lambda e, sc=sc, tc=tc, ic=ic, half=half, bv=bv: e.matmul(
                        pba[bv + half], xnfm[sc % 2][:, ic * 256 + tc * 128: ic * 256 + (tc + 1) * 128],
                        wsb2[:, ic * 1024 + half * 512: ic * 1024 + (half + 1) * 512],
                        start=(ic == 0), stop=(ic == ICH - 1)),
                        waits=wmmv if (half == 0 and ic == 0) else (), inc=(ts if ic == ICH - 1 else None))
            kv_last_mm[sc] = gv_ms[1]
            wev = [(ts, gk_ms[0]), (ds1, pbk)]
            if kc >= 2:
                wev.append((ss, sigk_ms[kc - 2]))
            P.do("vector", lambda e, kc=kc, bk=bk: e.tensor_tensor(
                scr1k[kc % 2][:, 0:512], pba[bk], pbias_b[:, 0:512], ALU.add), waits=wev, inc=vs)
            evk_ms[kc] = P.do("vector", lambda e, kc=kc, bk=bk: e.tensor_tensor(
                scr1k[kc % 2][:, 512:1024], pba[bk + 1], pbias_b[:, 512:1024], ALU.add),
                waits=[(ts, gk_ms[1])], inc=vs)
            sigk_ms[kc] = P.do("scalar", lambda e, kc=kc: e.activation(
                sigscr[kc % 2], scr1k[kc % 2], AF.Sigmoid, accum_out=skcol[:, kc:kc + 1]),
                waits=[(vs, evk_ms[kc])], inc=ss)
            wev = [(ts, gv_ms[0]), (ds1, pbv)]
            if kc >= 2:
                wev.append((ss, sigv_ms[kc - 2]))
            P.do("vector", lambda e, kc=kc, bv=bv: e.tensor_tensor(
                vevac[kc % 2][:, 0:512], pba[bv], wpre_b[:, 0:512], ALU.add), waits=wev, inc=vs)
            evv_ms[kc] = P.do("vector", lambda e, kc=kc, bv=bv: e.tensor_tensor(
                vevac[kc % 2][:, 512:1024], pba[bv + 1], wpre_b[:, 512:1024], ALU.add),
                waits=[(ts, gv_ms[1])], inc=vs)
            wsv = [(vs, evv_ms[kc])]
            if kc == 56:
                wsv.append((vs, norm_ms[KCH - 1]))
                wsv.append((ss, sq_ms[KCH - 1]))
            sigv_ms[kc] = P.do("scalar", lambda e, kc=kc: e.activation(
                bigbuf[:, kc * 1024:(kc + 1) * 1024], vevac[kc % 2], AF.Sigmoid),
                waits=wsv, inc=ss)
    a1b_done_ss = ss.n
    a1b_done_ts = ts.n

    if stage == "a1b":
        d1 = P.do("sync", lambda e: e.dma_start(out=dbg["skcol"], in_=skcol),
                  waits=[(ss, a1b_done_ss)], inc=osem, inck=16)
        P.do("sync", lambda e: e.dma_start(out=dbg["sqcol"], in_=sqcol), inc=osem, inck=16)
        P.do("sync", lambda e: e.dma_start(out=dbg["vsb"][:, 0:D], in_=bigbuf[:, 0:D]), inc=osem, inck=16)
        P.do("sync", lambda e: e.dma_start(out=dbg["vsb"][:, D:2 * D], in_=bigbuf[:, 63 * D:64 * D]), inc=osem, inck=16)
        P.do("sync", lambda e: e.dma_start(out=dbg["vev"], in_=vevac[1]), inc=osem, inck=16)
        P.do("sync", lambda e: e.dma_start(out=dbg["pbv"], in_=wpre_b), inc=osem, inck=16)
        P.wait("sync", (osem, osem.n))
        finish(nc, st, P)
        return nc


    A.reset(pool_a1b)
    A.reset(pool_mark)
    avsb = A.bf(4096)
    sqb = [A.f32(512), A.f32(512)]
    xres = [A.f32(1024), A.f32(1024)]
    bo_b = A.f32(1024)
    sqT = xres[1]
    pool_a2 = A.mark()
    dt = [scorep[:, 0:512], scorep[:, 512:1024], scorep[:, 1024:1536]]
    g2t = [scorep[:, 1536:2048], scorep[:, 2048:2560], scorep[:, 2560:3072]]
    pt = [wtg[1][:, 6144:6656], wtg[1][:, 6656:7168], wtg[1][:, 7168:7680]]

    if stage == "dbgv":
        P.do("sync", lambda e: e.dma_start(out=dbg["vsb0"], in_=bigbuf[:, 0:D]),
             waits=[(ss, a1b_done_ss)], inc=ds1, inck=16)
        for kc in range(KCH):
            P.do("sync", lambda e, kc=kc: e.dma_start(
                out=dbg["v4"][:, kc * 128:(kc + 1) * 128],
                in_=bigbuf[:, kc * 1024 + 512: kc * 1024 + 640]), inc=ds1, inck=16)
        P.do("sync", lambda e: e.dma_start(out=dbg["pbv"], in_=wpre_b), inc=ds1, inck=16)
    dwo = load_w(wsb, wo_i, wait_extra=[(ts, a1b_done_ts)])
    t_sq = P.do("tensor", lambda e: e.transpose(pba[7][0:32, 0:128], sqcol, eyef),
                waits=[(ss, sig_q_done), (vs, vs_scols), (vs, evv_ms[KCH - 1])], inc=ts)
    v_sq = P.do("vector", lambda e: e.tensor_copy(sqT[0:32, 0:128], pba[7][0:32, 0:128]),
                waits=[(ts, t_sq)], inc=vs)
    g_sq = P.do("gpsimd", lambda e: e.dma_start(
        out=sq_d.rearrange("a (b c) -> (a b) c", b=32), in_=sqT[0:32, 0:128]),
        waits=[(vs, v_sq)], inc=gs1, inck=16)
    emit_casts(["h2", "w3"])
    cs_mm = 0
    for ic in range(ICH):
        for kc in range(KCH):
            cs_mm = P.do("tensor", lambda e, kc=kc, ic=ic: e.matmul(
                pba[7][:, ic:ic + 1],
                bigbuf[:, kc * 1024 + ic * 128: kc * 1024 + (ic + 1) * 128],
                onescol_bf[:, 0:1],
                start=(kc == 0), stop=(kc == KCH - 1)),
                waits=[(ss, a1b_done_ss), (vs, v_sq)] if (kc == 0 and ic == 0) else (),
                inc=(ts if kc == KCH - 1 else None))
    P.do("vector", lambda e: e.tensor_single_scalar(cs09col, pba[7][:, 0:8], 0.9, ALU.mult),
         waits=[(ts, cs_mm)], inc=vs)
    vs_cs09 = vs.n

    csn_f = smalls[:, 360:368]
    cslo_f = smalls[:, 368:376]
    csbf = scorep[:, 2048:2056]
    v1c = P.do("vector", lambda e: e.tensor_single_scalar(csn_f, cs09col, -1.0, ALU.mult),
               waits=[(vs, vs_cs09)], inc=vs)
    v2c = P.do("vector", lambda e: e.tensor_copy(csbf, csn_f), waits=[(vs, v1c)], inc=vs)
    v3c = P.do("vector", lambda e: e.tensor_tensor(cslo_f, csn_f, csbf, ALU.subtract),
               waits=[(vs, v2c)], inc=vs)
    onesbig = sigdump[:, 0:512]
    v4c = P.do("vector", lambda e: e.memset(onesbig, 1.0), waits=[(ss, a1b_done_ss)], inc=vs)
    diagH = [wtg[1][:, 4096 + ic * 128: 4096 + (ic + 1) * 128] for ic in range(ICH)]
    diagL = [wtg[1][:, 5120 + ic * 128: 5120 + (ic + 1) * 128] for ic in range(ICH)]
    diag_ms = 0
    for ic in range(ICH):
        P.do("vector", lambda e, ic=ic: e.tensor_single_scalar(
            diagH[ic], eyef, csn_f[:, ic:ic + 1], ALU.mult), waits=[(vs, v3c)] if ic == 0 else (), inc=vs)
        diag_ms = P.do("vector", lambda e, ic=ic: e.tensor_single_scalar(
            diagL[ic], eyef, cslo_f[:, ic:ic + 1], ALU.mult), inc=vs)
    bob = fill_bcast(bo_b, bo_i[0:1, :], [(ss, a1b_done_ss)])

    NQB = 1 if stage == "a2d" else NO // 512
    sqb_ms = {}
    a2sq_ms = {}
    v5_ms = {}
    exp_ms = {}
    sct_ms = {}
    mm8_ms = {}
    avdone = {}
    avevac = {}
    attn_evac = {}
    wo_last = {}
    o1_wr = [0] * QCH
    out1_vms = [0] * QCH
    sq_last_d = {}
    av_hi_prev = 0
    for qb in range(NQB):
        w = [(gs1, g_sq)]
        if qb >= 2:
            w.append((vs, sq_last_d[qb - 2]))
        sqb_ms[qb] = P.do("sync", lambda e, qb=qb: e.dma_start(
            out=sqb[qb % 2], in_=sq_d[0:1, qb * 512:(qb + 1) * 512].broadcast_to([128, 512])),
            waits=w, inc=ds1, inck=16)
        for kc in range(KCH):
            wd = [(ds1, sqb_ms[qb])]
            gi = qb * KCH + kc
            if gi == 0:
                wd.append((ss, a1b_done_ss))
            if gi >= 3:
                wd.append((ss, exp_ms[gi - 3]))
            v_d = P.do("vector", lambda e, qb=qb, kc=kc, gi=gi: e.tensor_scalar(
                dt[gi % 3], sqb[qb % 2], skcol[:, kc:kc + 1], -1.0 / D, ALU.subtract, ALU.mult),
                waits=wd, inc=vs)
            if kc == KCH - 1:
                sq_last_d[qb] = v_d
            wg = [(vs, v_d)]
            if gi >= 3:
                wg.append((ts, mm8_ms[gi - 3]))
            v_g = P.do("vector", lambda e, gi=gi: e.tensor_scalar(
                g2t[gi % 3], dt[gi % 3], 0.0, 2.0, ALU.is_ge, ALU.mult), waits=wg, inc=vs)
            v_a = P.do("vector", lambda e, gi=gi: e.scalar_tensor_tensor(
                dt[gi % 3], g2t[gi % 3], 1.0, dt[gi % 3], ALU.subtract, ALU.mult),
                waits=[(vs, v_g)], inc=vs)
            we = [(vs, v_a)]
            if gi >= 3:
                we.append((vs, sct_ms[gi - 3]))
            exp_ms[gi] = P.do("scalar", lambda e, gi=gi: e.activation(
                pt[gi % 3], dt[gi % 3], AF.Exp, scale=-2.0, bias=ln09col[:, 0:1]), waits=we, inc=ss)
            sct_ms[gi] = P.do("vector", lambda e, gi=gi: e.scalar_tensor_tensor(
                g2t[gi % 3], g2t[gi % 3], 1.0, pt[gi % 3], ALU.subtract, ALU.mult),
                waits=[(ss, exp_ms[gi])], inc=vs)
            wm = [(vs, sct_ms[gi])]
            if kc == 0:
                if qb >= 1:
                    wm.append((vs, attn_evac[qb - 1]))
                    wm.append((vs, avevac[qb - 1]))
                    wm.append((ss, av_hi_prev))
            if kc == 0:
                for ic in range(ICH):
                    P.do("tensor", lambda e, ic=ic: e.matmul(
                        pba[ic], diagH[ic], onesbig, start=True, stop=False),
                        waits=([(vs, diag_ms), (vs, v4c)] if ic == 0 else ()) + (wm if ic == 0 else ()))
                    P.do("tensor", lambda e, ic=ic: e.matmul(
                        pba[ic], diagL[ic], onesbig, start=False, stop=False))
            for ic in range(ICH):
                mm8_ms[gi] = P.do("tensor", lambda e, kc=kc, ic=ic, gi=gi: e.matmul(
                    pba[ic], bigbuf[:, kc * 1024 + ic * 128: kc * 1024 + (ic + 1) * 128],
                    g2t[gi % 3], start=False, stop=(kc == KCH - 1)),
                    waits=wm if (ic == 0 and kc != 0) else (), inc=(ts if ic == ICH - 1 else None))
        avdone[qb] = ts.n
        av_hi_prev_next = None
        avlo = wtg[1][:, 0:4096]
        wv_ = [(ts, avdone[qb])]
        if qb >= 1:
            wv_.append((ts, wo_last[qb - 1]))
        av_hi = [0] * ICH
        av_lo = [0] * ICH
        for ic in range(ICH):
            av_hi[ic] = P.do("scalar", lambda e, ic=ic: e.activation(
                avsb[:, ic * 512:(ic + 1) * 512], pba[ic], AF.Copy),
                waits=wv_ if ic == 0 else (), inc=ss)
            av_lo[ic] = P.do("vector", lambda e, ic=ic, avlo=avlo: e.tensor_tensor(
                avlo[:, ic * 512:(ic + 1) * 512], pba[ic], avsb[:, ic * 512:(ic + 1) * 512],
                ALU.subtract), waits=[(ss, av_hi[ic])], inc=vs)
        avevac[qb] = av_lo[ICH - 1]
        av_hi_prev = av_hi[ICH - 1]
        for t4 in range(4):
            qc = qb * 4 + t4
            wo_ms = [0, 0]
            for half in range(2):
                for part in range(2):
                    srcbuf = avsb if part == 0 else avlo
                    for ic in range(ICH):
                        wmm = []
                        if t4 == 0:
                            wmm = [(ss, av_hi[ic])] if part == 0 else [(vs, av_lo[ic])]
                            if part == 0 and ic == 0:
                                wmm.append((vs, av_lo[half]))
                        if t4 == 0 and half == 0 and part == 0 and ic == 0:
                            wmm.append((gs1, dwo))
                        if t4 >= 1 and half == 0 and part == 0 and ic == 0:
                            wmm.append((vs, attn_evac[qb]))
                        wo_ms[half] = P.do("tensor", lambda e, t4=t4, ic=ic, half=half, srcbuf=srcbuf, part=part: e.matmul(
                            pba[half], srcbuf[:, ic * 512 + t4 * 128: ic * 512 + (t4 + 1) * 128],
                            wsb[:, ic * 1024 + half * 512: ic * 1024 + (half + 1) * 512],
                            start=(part == 0 and ic == 0), stop=(part == 1 and ic == ICH - 1)),
                            waits=wmm, inc=(ts if (part == 1 and ic == ICH - 1) else None))
            wo_last[qb] = wo_ms[1]
            wx = []
            if qc >= 2:
                wx.append((gs1, o1_wr[qc - 2]))

            if qc == 1:
                wx.append((gs1, g_sq))
            x_ms = P.do("sync", lambda e, qc=qc: e.dma_start(
                out=xres[qc % 2], in_=x_own[qc * 128:(qc + 1) * 128, :]), waits=wx, inc=ds1, inck=16)
            wev = [(ts, wo_ms[0]), (ds1, bob)]
            if qc >= 2:
                wev.append((vs, v5_ms[qc - 2]))
            P.do("vector", lambda e, qc=qc: e.tensor_tensor(
                scr1k[qc % 2][:, 0:512], bo_b[:, 0:512], pba[0], ALU.subtract), waits=wev, inc=vs)
            at_ms = P.do("vector", lambda e, qc=qc: e.tensor_tensor(
                scr1k[qc % 2][:, 512:1024], bo_b[:, 512:1024], pba[1], ALU.subtract),
                waits=[(ts, wo_ms[1])], inc=vs)
            lc = lncol[qc % 2]
            wsq = [(vs, at_ms)]
            if qc >= 1:
                wsq.append((vs, out1_vms[qc - 1]))
            a2sq_ms[qc] = P.do("scalar", lambda e, qc=qc, lc=lc: e.activation(
                sqscr, scr1k[qc % 2], AF.Square, accum_out=lc[:, 1:2]), waits=wsq, inc=ss)
            v1 = P.do("vector", lambda e, qc=qc, lc=lc: e.reduce_sum(lc[:, 0:1], scr1k[qc % 2], AX.X),
                      waits=[(vs, at_ms)], inc=vs)
            v2 = P.do("vector", lambda e, lc=lc: e.tensor_single_scalar(lc[:, 2:3], lc[:, 0:1], 1.0 / D, ALU.mult),
                      waits=[(vs, v1)], inc=vs)
            v3 = P.do("vector", lambda e, lc=lc: e.tensor_tensor(lc[:, 3:4], lc[:, 2:3], lc[:, 2:3], ALU.mult),
                      waits=[(vs, v2)], inc=vs)
            v4 = P.do("vector", lambda e, lc=lc: e.scalar_tensor_tensor(
                lc[:, 4:5], lc[:, 1:2], 1.0 / D, lc[:, 3:4], ALU.mult, ALU.subtract),
                waits=[(vs, v3), (ss, a2sq_ms[qc])], inc=vs)
            c1 = P.do("scalar", lambda e, lc=lc: e.activation(lc[:, 5:6], lc[:, 4:5], AF.Ln, bias=epscol[:, 0:1]),
                      waits=[(vs, v4)], inc=ss)
            c2 = P.do("scalar", lambda e, lc=lc: e.activation(lc[:, 6:7], lc[:, 5:6], AF.Exp, scale=-0.5),
                      waits=[(ss, c1)], inc=ss)
            v5 = P.do("vector", lambda e, qc=qc, lc=lc: e.tensor_scalar(
                sqscr, scr1k[qc % 2], lc[:, 2:3], lc[:, 6:7], ALU.subtract, ALU.mult),
                waits=[(ss, c2)], inc=vs)
            v5_ms[qc] = v5
            out1_vms[qc] = P.do("vector", lambda e, qc=qc: e.tensor_tensor(
                xres[qc % 2], sqscr, xres[qc % 2], ALU.add),
                waits=[(vs, v5), (ds1, x_ms)], inc=vs)
            attn_evac[qb] = out1_vms[qc]
            o1_wr[qc] = P.do("gpsimd", lambda e, qc=qc: e.dma_start(
                out=out1_d[qc * 128:(qc + 1) * 128, :], in_=xres[qc % 2]),
                waits=[(vs, out1_vms[qc])], inc=gs1, inck=16)
    a2_done = gs1.n

    if stage == "a2d":
        P.do("sync", lambda e: e.dma_start(out=dbg["avsb"], in_=avsb),
             waits=[(gs1, o1_wr[3])], inc=osem, inck=16)
        P.do("sync", lambda e: e.dma_start(out=dbg["sct63"], in_=g2t[1]), inc=osem, inck=16)
        P.do("sync", lambda e: e.dma_start(out=dbg["absd63"], in_=dt[1]), inc=osem, inck=16)
        P.do("sync", lambda e: e.dma_start(out=dbg["p63"], in_=pt[1]), inc=osem, inck=16)
        P.do("sync", lambda e: e.dma_start(out=dbg["sqb0"], in_=sqb[0]), inc=osem, inck=16)
        P.do("sync", lambda e: e.dma_start(out=dbg["sqrow"], in_=sq_d), inc=osem, inck=16)
        P.do("sync", lambda e: e.dma_start(out=dbg["woffbo"], in_=bo_b), inc=osem, inck=16)
        for i in range(4):
            d1 = P.do("sync", lambda e, i=i: e.dma_start(out=xres[i % 2], in_=out1_d[i * 128:(i + 1) * 128, :]),
                      waits=[(osem, osem.n)], inc=ds1, inck=16)
            P.do("sync", lambda e, i=i: e.dma_start(out=dbg["out1"][i * 128:(i + 1) * 128, :], in_=xres[i % 2]),
                 waits=[(ds1, d1)], inc=osem, inck=16)
        P.wait("sync", (osem, osem.n))
        finish(nc, st, P)
        return nc

    if stage == "a2":
        for i in range(QCH):
            d1 = P.do("sync", lambda e, i=i: e.dma_start(out=xres[i % 2], in_=out1_d[i * 128:(i + 1) * 128, :]),
                      waits=[(gs1, o1_wr[QCH - 1])] + ([(osem, 16 * (i - 1))] if i >= 2 else []), inc=ds1, inck=16)
            P.do("sync", lambda e, i=i: e.dma_start(out=dbg["out1"][i * 128:(i + 1) * 128, :], in_=xres[i % 2]),
                 waits=[(ds1, d1)], inc=osem, inck=16)
        P.wait("sync", (osem, osem.n))
        finish(nc, st, P)
        return nc


    xn3t_s = A.bf(1024)
    xn3t = [xn3t_s, xn3t_s]
    pool_b0 = A.mark()

    xn3_wr = [0] * QCH
    for j in range(QCH):
        wl = [(gs1, o1_wr[j]), (vs, v5_ms[30 + (j % 2)])]
        if j >= 2:
            wl.append((vs, b0n_ms[j - 2]))
            wl.append((ss, b0sq_ms[j - 2]))
        ld = P.do("sync", lambda e, j=j: e.dma_start(out=scr1k[j % 2], in_=out1_d[j * 128:(j + 1) * 128, :]),
                  waits=wl, inc=ds2, inck=16)
        sc = statc[j % 2]
        wsq = [(ds2, ld)]
        if j >= 1:
            wsq.append((vs, b0n_ms[j - 1]))
            wsq.append((gs2, xn3_wr[j - 1]))
        b0sq = P.do("scalar", lambda e, j=j, sc=sc: e.activation(
            xn3t[j % 2], scr1k[j % 2], AF.Square, accum_out=sc[:, 1:2]), waits=wsq, inc=ss)
        v1 = P.do("vector", lambda e, j=j, sc=sc: e.reduce_sum(sc[:, 0:1], scr1k[j % 2], AX.X),
                  waits=[(ds2, ld)] + ([(ss, b0sq_ms[j - 2])] if j >= 2 else []), inc=vs)
        v2 = P.do("vector", lambda e, sc=sc: e.tensor_single_scalar(sc[:, 2:3], sc[:, 0:1], 1.0 / D, ALU.mult),
                  waits=[(vs, v1)], inc=vs)
        v3 = P.do("vector", lambda e, sc=sc: e.tensor_tensor(sc[:, 3:4], sc[:, 2:3], sc[:, 2:3], ALU.mult),
                  waits=[(vs, v2)], inc=vs)
        v4 = P.do("vector", lambda e, sc=sc: e.scalar_tensor_tensor(
            sc[:, 4:5], sc[:, 1:2], 1.0 / D, sc[:, 3:4], ALU.mult, ALU.subtract),
            waits=[(vs, v3), (ss, b0sq)], inc=vs)
        c1 = P.do("scalar", lambda e, sc=sc: e.activation(sc[:, 5:6], sc[:, 4:5], AF.Ln, bias=epscol[:, 0:1]),
                  waits=[(vs, v4)], inc=ss)
        c2 = P.do("scalar", lambda e, sc=sc: e.activation(sc[:, 6:7], sc[:, 5:6], AF.Exp, scale=-0.5),
                  waits=[(ss, c1)], inc=ss)
        nm = P.do("vector", lambda e, j=j, sc=sc: e.tensor_scalar(
            xn3t[j % 2], scr1k[j % 2], sc[:, 2:3], sc[:, 6:7], ALU.subtract, ALU.mult),
            waits=[(ss, c2)], inc=vs)
        if j == 0:
            b0n_ms = [0] * QCH
            b0sq_ms = [0] * QCH
        b0n_ms[j] = nm
        b0sq_ms[j] = b0sq
        xn3_wr[j] = P.do("gpsimd", lambda e, j=j: e.dma_start(
            out=xn3_d[j * 128:(j + 1) * 128, :], in_=xn3t[j % 2]), waits=[(vs, nm)], inc=gs2, inck=16)

    xn3fm_tb = [0] * 8
    for tb in range(8):
        for ic in range(ICH):
            w = [(gs2, xn3_wr[tb * 4 + 3])]
            if ic == 0 and tb == 0:
                w.append((ts, avdone[NQB - 1]))
            xn3fm_tb[tb] = P.do("sync", lambda e, ic=ic, tb=tb: e.dma_start(
                out=bigbuf[:, ic * 4096 + tb * 512: ic * 4096 + (tb + 1) * 512],
                in_=xn3_d[tb * 512:(tb + 1) * 512, ic * 128:(ic + 1) * 128], transpose=True),
                waits=w, inc=ds2, inck=16)

    _avf = avsb.bitcast(F32)
    silsc = [_avf[:, 0:512], _avf[:, 512:1024]]
    houtt = [scorep[:, 1024:1536], scorep[:, 1536:2048]]
    goutt = [scorep[:, 2048:2560], scorep[:, 2560:3072]]

    b1_grp_mm = {}
    h_wr = [0] * 32
    h_ms = {}
    silu_ms = {}
    b1_wdma = {}
    b1_w1dma = {}
    h_wr_n = [0] * 256
    for grp in range(8):
        wld = [(gs1, cast_ms["w1"])]
        if grp == 0:
            wld.append((ts, wo_last[NQB - 1]))
        if grp == 1:
            wld.append((ts, wo_last[NQB - 1]))
        if grp >= 2:
            wld.append((ts, b1_grp_mm[grp - 2]))
        for ic in range(ICH):
            b1_w1dma[grp] = P.do("sync", lambda e, grp=grp, ic=ic: e.dma_start(
                out=wtg[grp % 2][:, ic * 512:(ic + 1) * 512],
                in_=w1c[grp * 512:(grp + 1) * 512, ic * 128:(ic + 1) * 128], transpose=True),
                waits=wld if ic == 0 else (), inc=ds2, inck=16)
        wld2 = [(gs1, cast_ms["w2"])] + wld[1:]
        for ic in range(ICH):
            b1_wdma[grp] = P.do("scalar" if SPLITQ else "sync", lambda e, grp=grp, ic=ic: e.dma_start(
                out=wtg[grp % 2][:, 4096 + ic * 512: 4096 + (ic + 1) * 512],
                in_=w2c[grp * 512:(grp + 1) * 512, ic * 128:(ic + 1) * 128], transpose=True),
                waits=wld2 if ic == 0 else (), inc=(sds if SPLITQ else ds2), inck=16)
        for tb in range(8):
            for hs_ in range(4):
                hs = grp * 4 + hs_
                n = grp * 32 + tb * 4 + hs_
                bk = (n % 2) * 2
                wmm = [(ds2, b1_w1dma[grp]), ((sds if SPLITQ else ds2), b1_wdma[grp]), (ds2, xn3fm_tb[tb])]
                if n >= 2:
                    wmm.append((vs, h_ms[n - 2]))
                ga = [0, 0]
                for br in range(2):
                    for ic in range(ICH):
                        ga[br] = P.do("tensor", lambda e, grp=grp, hs_=hs_, tb=tb, ic=ic, br=br, bk=bk: e.matmul(
                            pba[bk + br], wtg[grp % 2][:, br * 4096 + ic * 512 + hs_ * 128: br * 4096 + ic * 512 + (hs_ + 1) * 128],
                            bigbuf[:, ic * 4096 + tb * 512:ic * 4096 + (tb + 1) * 512],
                            start=(ic == 0), stop=(ic == ICH - 1)),
                            waits=wmm if (br == 0 and ic == 0) else (), inc=(ts if ic == ICH - 1 else None))
                b1_grp_mm[grp] = ga[1]
                wsl = [(ts, ga[0])]
                if n >= 2:
                    wsl.append((vs, h_ms[n - 2]))
                silu_ms[n] = P.do("scalar", lambda e, n=n, hs=hs, bk=bk: e.activation(
                    silsc[n % 2], pba[bk], AF.Silu, scale=s1col[:, hs:hs + 1]), waits=wsl, inc=ss)
                wh = [(ts, ga[1]), (ss, silu_ms[n])]
                if n >= 2:
                    wh.append((gs2, h_wr_n[n - 2]))
                h_ms[n] = P.do("vector", lambda e, n=n, hs=hs, bk=bk: e.scalar_tensor_tensor(
                    houtt[n % 2], pba[bk + 1], s2col[:, hs:hs + 1], silsc[n % 2], ALU.mult, ALU.mult),
                    waits=wh, inc=vs)
                h_wr_n[n] = P.do("gpsimd", lambda e, n=n, hs=hs, tb=tb: e.dma_start(
                    out=h_d[hs * 128:(hs + 1) * 128, tb * 512:(tb + 1) * 512], in_=houtt[n % 2]),
                    waits=[(vs, h_ms[n])], inc=gs2, inck=16)
                if tb == 7:
                    h_wr[hs] = h_wr_n[n]
    b1_last_mm = b1_grp_mm[7]

    def emit_mid(src_d, wc_name, wc_d, scol, dst_d, src_wr, upper_gate, lower_gate):
        evac_ms = {}
        wr_ms = {}
        wt_mm = {}
        ld_blk = [0, 0, 0, 0]
        blk_last = [0, 0, 0, 0]
        last_mm = 0
        for pair in range(2):
            for blk in (2 * pair, 2 * pair + 1):
                base = 32768 if blk % 2 == 0 else 0
                for kc in range(32):
                    w = [(gs2, src_wr(kc, blk))]
                    if kc == 0:
                        if blk == 0:
                            w.append((ts, upper_gate))
                        elif blk == 1:
                            w.append((ts, lower_gate))
                        else:
                            w.append((ts, blk_last[blk - 2]))
                    ld_blk[blk] = P.do("gpsimd", lambda e, blk=blk, kc=kc, src_d=src_d, base=base: e.dma_start(
                        out=bigbuf[:, base + kc * 1024: base + (kc + 1) * 1024],
                        in_=src_d[kc * 128:(kc + 1) * 128, blk * 1024:(blk + 1) * 1024]),
                        waits=w, inc=gs2, inck=16)
            for ob2 in range(16):
                nob = pair * 16 + ob2
                wld = [(gs1, cast_ms[wc_name])]
                if nob == 0:
                    wld.append((ts, lower_gate))
                if nob >= 2:
                    wld.append((ts, wt_mm[nob - 2]))
                qeng = "sync" if (ob2 % 2 == 0 or not SPLITQ) else "scalar"
                qsem = ds2 if (ob2 % 2 == 0 or not SPLITQ) else sds
                for kc in range(32):
                    P.do(qeng, lambda e, ob2=ob2, kc=kc, wc_d=wc_d: e.dma_start(
                        out=wtg[ob2 % 2][:, kc * 256:(kc + 1) * 256],
                        in_=wc_d[ob2 * 256:(ob2 + 1) * 256, kc * 128:(kc + 1) * 128], transpose=True),
                        waits=wld if kc == 0 else (), inc=qsem, inck=16)
                wt_dma = (qsem, qsem.n)
                for blkk in range(2):
                    blk = 2 * pair + blkk
                    base = 32768 if blk % 2 == 0 else 0
                    for os_ in range(2):
                        for ts2 in range(2):
                            nn = (((nob * 2 + blkk) * 2 + os_) * 2 + ts2)
                            bkx = nn % 4
                            wmm = [wt_dma, (gs2, ld_blk[blk])]
                            if nn >= 4:
                                wmm.append((ss, evac_ms[nn - 4]))
                            for kc in range(32):
                                last_mm = P.do("tensor", lambda e, ob2=ob2, os_=os_, ts2=ts2, kc=kc, bkx=bkx, base=base: e.matmul(
                                    pba[bkx], wtg[ob2 % 2][:, kc * 256 + os_ * 128: kc * 256 + (os_ + 1) * 128],
                                    bigbuf[:, base + kc * 1024 + ts2 * 512: base + kc * 1024 + (ts2 + 1) * 512],
                                    start=(kc == 0), stop=(kc == 31)),
                                    waits=wmm if kc == 0 else (), inc=(ts if kc == 31 else None))
                            wt_mm[nob] = last_mm
                            blk_last[blk] = last_mm
                            o_abs = ob2 * 256 + os_ * 128
                            wev = [(ts, last_mm)]
                            if nn >= 2:
                                wev.append((gs2, wr_ms[nn - 2]))
                            evac_ms[nn] = P.do("scalar", lambda e, nn=nn, o_abs=o_abs, bkx=bkx, scol=scol: e.activation(
                                goutt[nn % 2], pba[bkx], AF.Gelu, scale=scol[:, o_abs // 128:o_abs // 128 + 1]),
                                waits=wev, inc=ss)
                            wr_ms[nn] = P.do("gpsimd", lambda e, nn=nn, o_abs=o_abs, blk=blk, ts2=ts2, dst_d=dst_d: e.dma_start(
                                out=dst_d[o_abs:o_abs + 128, blk * 1024 + ts2 * 512: blk * 1024 + (ts2 + 1) * 512],
                                in_=goutt[nn % 2]), waits=[(ss, evac_ms[nn])], inc=gs2, inck=16)
        def dst_wr_fn(kc, blk2, ts2=1, wr_ms=wr_ms):
            nn = ((((blk2 // 2) * 16 + kc // 2) * 2 + (blk2 % 2)) * 2 + (kc % 2)) * 2 + ts2
            return wr_ms[nn]
        return blk_last, dst_wr_fn

    def h_wr_fn(kc, blk):
        return h_wr_n[(kc // 4) * 32 + (2 * blk + 1) * 4 + (kc % 4)]
    b2_blk, g_wr = emit_mid(h_d, "h1", h1c, sh1col, g_d, h_wr_fn,
                            avdone[NQB - 1], b1_last_mm)
    b3_blk, k_wr = emit_mid(g_d, "h2", h2c, sh2col, k_d, g_wr,
                            b2_blk[2], b2_blk[3])
    b3_last_mm = b3_blk[3]

    A.reset(pool_mark)
    o1re = [A.f32(1024), A.f32(1024)]
    o2t = [A.f32(1024), A.f32(1024)]
    pool_b4 = A.mark()
    yfm = [bigbuf[:, 0:4096].bitcast(F32), bigbuf[:, 4096:8192].bitcast(F32)]

    w3t_done = 0
    for hc in range(32):
        for ob in range(2):
            w = [(gs1, cast_ms["w3"])]
            if hc == 0 and ob == 0:
                w.append((ts, b3_blk[2]))
            w3t_done = P.do("sync", lambda e, hc=hc, ob=ob: e.dma_start(
                out=bigbuf[:, 32768 + hc * 1024 + ob * 512: 32768 + hc * 1024 + (ob + 1) * 512],
                in_=w3c[ob * 512:(ob + 1) * 512, hc * 128:(hc + 1) * 128], transpose=True),
                waits=w, inc=ds2, inck=16)

    yev_ms = {}
    ytm_ms = {}
    out2_ms = {}
    b4sq_ms = [0] * 32
    tp_last = [0] * 16
    b4_mm_last = [0] * 16
    o2dma = [0] * QCH
    kk_ld = {}

    def b4_transpose_ln4(tb4):
        for tsub in range(2):
            qc = tb4 * 2 + tsub
            pbt = pba[4 + 2 * tsub]
            pbt2 = pba[5 + 2 * tsub]
            tpm = 0
            for os in range(8):
                wtp = [(vs, yev_ms[tb4 * 8 + 7])]
                if tb4 >= 1:
                    wtp.append((vs, ytm_ms[(tb4 - 1) * 2 + tsub]))
                tpm = P.do("tensor", lambda e, os=os, tsub=tsub, pbt=pbt, pbt2=pbt2, tb4=tb4: e.transpose(
                    (pbt if os < 4 else pbt2)[:, (os % 4) * 128:(os % 4 + 1) * 128],
                    yfm[tb4 % 2][:, os * 256 + tsub * 128: os * 256 + tsub * 128 + 128], eyef),
                    waits=wtp if os == 0 else (), inc=(ts if os == 7 else None))
            if tsub == 1:
                tp_last[tb4] = tpm
            wyt = [(ts, tpm)]
            if qc >= 2:
                wyt.append((ss, b4sq_ms[qc - 2]))
                wyt.append((vs, out2_ms[qc - 2]))
            P.do("vector", lambda e, qc=qc, pbt=pbt: e.tensor_copy(scr1k[qc % 2][:, 0:512], pbt[:, 0:512]),
                 waits=wyt, inc=vs)
            ytm_ms[qc] = P.do("vector", lambda e, qc=qc, pbt2=pbt2: e.tensor_copy(
                scr1k[qc % 2][:, 512:1024], pbt2[:, 0:512]), inc=vs)
            sc = lncol[qc % 2]
            wsq = [(vs, ytm_ms[qc])]
            if qc >= 1:
                wsq.append((vs, out2_ms[qc - 1]))
            bsq = P.do("scalar", lambda e, qc=qc, sc=sc: e.activation(
                sqscr, scr1k[qc % 2], AF.Square, accum_out=sc[:, 1:2]), waits=wsq, inc=ss)
            b4sq_ms[qc] = bsq
            v1 = P.do("vector", lambda e, qc=qc, sc=sc: e.reduce_sum(sc[:, 0:1], scr1k[qc % 2], AX.X),
                      waits=[(vs, ytm_ms[qc])], inc=vs)
            v2 = P.do("vector", lambda e, sc=sc: e.tensor_single_scalar(sc[:, 2:3], sc[:, 0:1], 1.0 / D, ALU.mult),
                      waits=[(vs, v1)], inc=vs)
            v3 = P.do("vector", lambda e, sc=sc: e.tensor_tensor(sc[:, 3:4], sc[:, 2:3], sc[:, 2:3], ALU.mult),
                      waits=[(vs, v2)], inc=vs)
            v4 = P.do("vector", lambda e, sc=sc: e.scalar_tensor_tensor(
                sc[:, 4:5], sc[:, 1:2], 1.0 / D, sc[:, 3:4], ALU.mult, ALU.subtract),
                waits=[(vs, v3), (ss, bsq)], inc=vs)
            c1 = P.do("scalar", lambda e, sc=sc: e.activation(sc[:, 5:6], sc[:, 4:5], AF.Ln, bias=epscol[:, 0:1]),
                      waits=[(vs, v4)], inc=ss)
            c2 = P.do("scalar", lambda e, sc=sc: e.activation(sc[:, 6:7], sc[:, 5:6], AF.Exp, scale=-0.5),
                      waits=[(ss, c1)], inc=ss)
            v5b = P.do("vector", lambda e, qc=qc, sc=sc: e.tensor_scalar(
                sqscr, scr1k[qc % 2], sc[:, 2:3], sc[:, 6:7], ALU.subtract, ALU.mult),
                waits=[(ss, c2)], inc=vs)
            wx = []
            if qc >= 2:
                wx.append((osem, 16 * (qc - 1)))
            xl = P.do("sync", lambda e, qc=qc: e.dma_start(
                out=o1re[qc % 2], in_=out1_d[qc * 128:(qc + 1) * 128, :]), waits=wx, inc=ds2, inck=16)
            out2_ms[qc] = P.do("vector", lambda e, qc=qc: e.tensor_tensor(
                o2t[qc % 2], sqscr, o1re[qc % 2], ALU.add), waits=[(vs, v5b), (ds2, xl)], inc=vs)
            o2dma[qc] = P.do("sync", lambda e, qc=qc: e.dma_start(
                out=out_o[qc * 128:(qc + 1) * 128, :], in_=o2t[qc % 2]),
                waits=[(vs, out2_ms[qc])], inc=osem, inck=16)

    for tb4 in range(16):
        w = [(ds2, w3t_done)]
        if tb4 >= 2:
            w.append((ts, b4_mm_last[tb4 - 2]))
        for kc in range(32):
            kk_ld[tb4] = P.do("gpsimd", lambda e, tb4=tb4, kc=kc: e.dma_start(
                out=wtg[tb4 % 2][:, kc * 256:(kc + 1) * 256],
                in_=k_d[kc * 128:(kc + 1) * 128, tb4 * 256:(tb4 + 1) * 256]),
                waits=(w if kc == 0 else []) + [(gs2, k_wr(kc, tb4 // 4, (tb4 % 4) // 2))], inc=gs2, inck=16)
        for os in range(8):
            nn = tb4 * 8 + os
            wmm = [(gs2, kk_ld[tb4])]
            if nn >= 2:
                wmm.append((vs, yev_ms[nn - 2]))
            mm_ms = 0
            for kc in range(32):
                mm_ms = P.do("tensor", lambda e, tb4=tb4, os=os, kc=kc: e.matmul(
                    pba[os % 2][:, 0:256], bigbuf[:, 32768 + kc * 1024 + os * 128: 32768 + kc * 1024 + (os + 1) * 128],
                    wtg[tb4 % 2][:, kc * 256:(kc + 1) * 256],
                    start=(kc == 0), stop=(kc == 31)),
                    waits=wmm if kc == 0 else (), inc=(ts if kc == 31 else None))
            wyv = [(ts, mm_ms)]
            if os == 0 and tb4 >= 2:
                wyv.append((ts, tp_last[tb4 - 2]))
            if os == 0 and tb4 < 2:
                wyv.append((ts, b3_last_mm))
            yev_ms[nn] = P.do("vector", lambda e, os=os, tb4=tb4: e.tensor_single_scalar(
                yfm[tb4 % 2][:, os * 256:(os + 1) * 256], pba[os % 2][:, 0:256], s3col[:, os:os + 1], ALU.mult),
                waits=wyv, inc=vs)
        b4_mm_last[tb4] = mm_ms
        if tb4 >= 1:
            b4_transpose_ln4(tb4 - 1)
    b4_transpose_ln4(15)
    P.wait("sync", (osem, 16 * QCH))

    if stage == "dbgv":
        P.do("sync", lambda e: e.dma_start(out=dbg["bo"], in_=bo_b), waits=[(osem, 16 * QCH)], inc=osem, inck=16)
        P.do("sync", lambda e: e.dma_start(out=dbg["cs"], in_=cs09col), inc=osem, inck=16)
        P.do("sync", lambda e: e.dma_start(out=dbg["sk"], in_=skcol), inc=osem, inck=16)
        P.do("sync", lambda e: e.dma_start(out=dbg["sq"], in_=sqcol), inc=osem, inck=16)
        P.wait("sync", (osem, osem.n))

    if stage == "bdbg":
        stg = o1re[0].bitcast(BF16)
        def dump(dst_ap, src_d, row0, ncols=NO, f32=False):
            step = min(2048, ncols) if not f32 else min(1024, ncols)
            sview = o1re[0] if f32 else stg
            for cc in range(ncols // step):
                d1 = P.do("sync", lambda e, src_d=src_d, row0=row0, cc=cc, step=step, sview=sview: e.dma_start(
                    out=sview[:, 0:step], in_=src_d[row0:row0 + 128, cc * step:(cc + 1) * step]),
                    waits=[(osem, osem.n)], inc=ds2, inck=16)
                P.do("sync", lambda e, dst_ap=dst_ap, cc=cc, step=step, sview=sview: e.dma_start(
                    out=dst_ap[:, cc * step:(cc + 1) * step], in_=sview[:, 0:step]),
                    waits=[(ds2, d1)], inc=osem, inck=16)
        dump(dbg["h"][0:128, :], h_d, 0)
        dump(dbg["h"][128:256, :], h_d, 2048)
        dump(dbg["g"][0:128, :], g_d, 0)
        dump(dbg["g"][128:256, :], g_d, 2048)
        dump(dbg["k"][0:128, :], k_d, 0)
        dump(dbg["k"][128:256, :], k_d, 2048)
        dump(dbg["xn3"], xn3_d, 0, 1024)
        dump(dbg["out1"][0:128, :], out1_d, 0, 1024, f32=True)
        dump(dbg["out1"][128:256, :], out1_d, 2048, 1024, f32=True)
        P.wait("sync", (osem, osem.n))

    finish(nc, st, P)
    return nc




def finish(nc, st, P):
    with nc.Block() as blk:
        blk.sync(lambda e: [f(e) for f in P.q["sync"]])
        blk.gpsimd(lambda e: [f(e) for f in P.q["gpsimd"]])
        blk.vector(lambda e: [f(e) for f in P.q["vector"]])
        blk.scalar(lambda e: [f(e) for f in P.q["scalar"]])
        blk.tensor(lambda e: [f(e) for f in P.q["tensor"]])
    st.close()
    print(f"build: {P.ninst} emitted units")


_NC_CACHE = {}


def _get_nc():
    if "nc" not in _NC_CACHE:
        _NC_CACHE["nc"] = build("full")
    return _NC_CACHE["nc"]


def kernel(**inp):
    x = np.ascontiguousarray(inp["L0_attn_x"])
    base = {
        "wq": inp["L0_attn_Wq"], "wk": inp["L0_attn_Wk"],
        "wv": inp["L0_attn_Wv"], "wo": inp["L0_attn_Wo"],
        "bo": inp["L0_attn_bo"].reshape(1, D),
        "w1": inp["L0_ffn_w1_w"], "w2": inp["L0_ffn_w2_w"],
        "h1": inp["L0_ffn_h1_w"], "h2": inp["L0_ffn_h2_w"],
        "w3": inp["L0_ffn_w3_w"],
        "s1": inp["L0_ffn_w1_s"].reshape(32, 128),
        "s2": inp["L0_ffn_w2_s"].reshape(32, 128),
        "sh1": inp["L0_ffn_h1_s"].reshape(32, 128),
        "sh2": inp["L0_ffn_h2_s"].reshape(32, 128),
        "s3": inp["L0_ffn_w3_s"].reshape(8, 128),
    }
    for j, J in (("q", "Wq"), ("k", "Wk"), ("v", "Wv")):
        for p in ("g", "b", "m", "v"):
            base[p + j] = inp[f"L0_attn_{p}{j}"].reshape(1, D)
    in_maps = []
    for c in range(8):
        b, h = c // 2, c % 2
        m = dict(base)
        m["x_own"] = x[b, h * NO:(h + 1) * NO]
        m["x_oth"] = x[b, (1 - h) * NO:(2 - h) * NO]
        in_maps.append(m)
    nc = _get_nc()
    res = run_bass_kernel_spmd(nc, in_maps, core_ids=list(range(8)))
    out = np.empty((4, NB, D), np.float32)
    for c in range(8):
        b, h = c // 2, c % 2
        out[b, h * NO:(h + 1) * NO] = np.asarray(res.results[c]["out"], np.float32)
    return out
